# version 03
# baseline (speedup 1.0000x reference)
import math
import numpy as np
import ml_dtypes
from contextlib import ExitStack

import concourse.bass as bass
import concourse.mybir as mybir
from concourse.bass_utils import run_bass_kernel_spmd

F32 = mybir.dt.float32
BF16 = mybir.dt.bfloat16
AF = mybir.ActivationFunctionType
ALU = mybir.AluOpType
AX = mybir.AxisListType
BF = ml_dtypes.bfloat16

DEBUG = False
S, D, H, HKV, HD = 2048, 4096, 32, 8, 128
NQT = S // 128
KEEP = 205
R_ROUNDS = 10
LNTGT1 = float(np.float32(math.log(201.0)))
LNTGT2 = float(np.float32(math.log(203.0)))
BIGNEG = float(np.float32(-3.0e38))
MASKC = float(np.float32(2.0 ** 100))
RMASKC = float(np.float32(2.0 ** -100))
SQRT_HD = float(np.float32(np.sqrt(np.float64(HD))))

F32_COLS = 14336
BF_COLS = 58368

HALVES = ((0, 10), (10, 6))


def _norm_ppf(p):
    a = [-3.969683028665376e+01, 2.209460984245205e+02, -2.759285104469687e+02,
         1.383577518672690e+02, -3.066479806614716e+01, 2.506628277459239e+00]
    b = [-5.447609879822406e+01, 1.615858368580409e+02, -1.556989798598866e+02,
         6.680131188771972e+01, -1.328068155288572e+01]
    c = [-7.784894002430293e-03, -3.223964580411365e-01, -2.400758277161838e+00,
         -2.549732539343734e+00, 4.374664141464968e+00, 2.938163982698783e+00]
    d = [7.784695709041462e-03, 3.224671290700398e-01, 2.445134137142996e+00,
         3.754408661907416e+00]
    p = np.clip(np.asarray(p, dtype=np.float64), 1e-12, 1 - 1e-12)
    out = np.empty_like(p)
    lo_m = p < 0.02425
    hi_m = p > 1 - 0.02425
    mid = ~(lo_m | hi_m)
    if lo_m.any():
        q = np.sqrt(-2 * np.log(p[lo_m]))
        out[lo_m] = (((((c[0]*q+c[1])*q+c[2])*q+c[3])*q+c[4])*q+c[5]) / \
                    ((((d[0]*q+d[1])*q+d[2])*q+d[3])*q+1)
    if hi_m.any():
        q = np.sqrt(-2 * np.log(1 - p[hi_m]))
        out[hi_m] = -(((((c[0]*q+c[1])*q+c[2])*q+c[3])*q+c[4])*q+c[5]) / \
                    ((((d[0]*q+d[1])*q+d[2])*q+d[3])*q+1)
    if mid.any():
        q = p[mid] - 0.5
        r = q * q
        out[mid] = (((((a[0]*r+a[1])*r+a[2])*r+a[3])*r+a[4])*r+a[5])*q / \
                   (((((b[0]*r+b[1])*r+b[2])*r+b[3])*r+b[4])*r+1)
    return out


def _hazard(z):
    z = np.asarray(z, dtype=np.float64)
    cdf = np.array([0.5 * (1 + math.erf(v / math.sqrt(2)))
                    for v in z.ravel()]).reshape(z.shape)
    pdf = np.exp(-0.5 * z * z) / math.sqrt(2 * math.pi)
    return pdf / np.maximum(1 - cdf, 1e-300)


def _host_consts():
    cst = {}
    import jax
    import jax.numpy as jnp
    with jax.default_device(jax.devices("cpu")[0]):
        inv_freq = 1.0 / 10000.0 ** (jnp.arange(0, HD, 2, dtype=jnp.float32)
                                     / HD)
        t = jnp.arange(S, dtype=jnp.float32)
        freqs = jnp.outer(t, inv_freq)
        emb = jnp.concatenate([freqs, freqs], axis=-1)
        cos = np.asarray(jnp.cos(emb))
        sin = np.asarray(jnp.sin(emb))
    cst["cost"] = np.ascontiguousarray(cos.T)
    cst["sint"] = np.ascontiguousarray(sin.T)
    P = np.zeros((128, 128), np.float32)
    for dp in range(64):
        P[dp, dp + 64] = -1.0
        P[dp + 64, dp] = 1.0
    cst["protT"] = np.ascontiguousarray(P.T)
    cst["ident_bf"] = np.eye(128, dtype=BF)
    cst["ident_f32"] = np.eye(128, dtype=np.float32)
    cst["diagm"] = np.where(np.arange(128)[:, None] >= np.arange(128)[None, :],
                            np.float32(0), np.float32(BIGNEG)).astype(np.float32)
    cst["iota8"] = np.broadcast_to(np.arange(8, dtype=np.float32), (128, 8)).copy()
    cst["ones_bf"] = np.ones((1, 128), dtype=BF)
    cst["ones_f"] = np.ones((1, 128), dtype=np.float32)
    cst["s32col"] = np.full((128, 1), SQRT_HD, np.float32)
    cst["c20col"] = np.full((128, 1), np.float32(20.0), np.float32)
    qg = np.arange(128)[:, None] + 128 * np.arange(NQT)[None, :]
    n = (qg + 1).astype(np.float64)
    Lk = (128.0 * (np.arange(NQT) + 1))
    z = _norm_ppf(1.0 - KEEP / np.maximum(n, KEEP))
    cst["z_t"] = z.astype(np.float32)
    cst["hz_t"] = _hazard(z).astype(np.float32)
    cst["n_t"] = n.astype(np.float32)
    cst["invLk_t"] = np.broadcast_to((1.0 / Lk), (128, NQT)).astype(np.float32)
    cst["Lk2_t"] = np.broadcast_to(Lk * 0.5, (128, NQT)).astype(np.float32)
    cst["minflag_t"] = ((n >= KEEP) & (n <= KEEP + 7)).astype(np.float32)
    cst["minidx_t"] = np.clip(n - KEEP, 0, 7).astype(np.float32)
    return cst


class Ctr:
    def __init__(self, stack, nc, name):
        self.h = stack.enter_context(nc.semaphore(name))
        self.n = 0

    def tick(self, instr, k=1):
        instr.then_inc(self.h, k)
        self.n += k
        return self.n


def build_program():
    nc = bass.Bass("TRN2", target_bir_lowering=False, debug=False)

    def din(name, shape, dt=F32):
        return nc.dram_tensor(name, shape, dt, kind="ExternalInput").ap()

    x_h = din("x", [S, D])
    wqt_h = din("wqt", [D, D])
    wkt_h = din("wkt", [D, HKV * HD])
    wvt_h = din("wvt", [D, HKV * HD])
    wot_h = din("wot", [D, D])
    wft_h = din("wft", [D, D])
    fb_h = din("fb", [128, 32])
    wsc_h = din("wsc", [1, 1])
    bsc_h = din("bsc", [1, 1])
    cost_h = din("cost", [128, S])
    sint_h = din("sint", [128, S])
    identb_h = din("ident_bf", [128, 128], BF16)
    identf_h = din("ident_f32", [128, 128])
    diagm_h = din("diagm", [128, 128])
    iota8_h = din("iota8", [128, 8])
    onesb_h = din("ones_bf", [1, 128], BF16)
    onesf_h = din("ones_f", [1, 128])
    s32_h = din("s32col", [128, 1])
    prot_h = din("protT", [128, 128])
    c20_h = din("c20col", [128, 1])
    lits_h = din("lits", [128, 2])
    tabs_h = {nm: din(nm, [128, NQT]) for nm in
              ("z_t", "hz_t", "n_t", "invLk_t", "Lk2_t", "minflag_t", "minidx_t")}

    out_h = nc.dram_tensor("out", [S, D], F32, kind="ExternalOutput").ap()
    _ik = "ExternalOutput" if DEBUG else "Internal"
    if DEBUG:
        dbgst_h = nc.dram_tensor("dbgst", [H, 2, 128, 512], F32,
                                 kind="ExternalOutput").ap()
    xnsp_h = nc.dram_tensor("xnsp", [32, 128, S], BF16, kind=_ik).ap()
    qsp_h = nc.dram_tensor("qsp", [H, 128, S], BF16, kind=_ik).ap()
    atsp_h = nc.dram_tensor("atsp", [32, 128, S], BF16, kind=_ik).ap()
    btb_h = nc.dram_tensor("btb", [32, 128], BF16, kind="Internal").ap()

    stack = ExitStack()
    AF_ = stack.enter_context(nc.sbuf_tensor("arena_f", [128, F32_COLS], F32))
    AB_ = stack.enter_context(nc.sbuf_tensor("arena_b", [128, BF_COLS], BF16))
    st = stack.enter_context(nc.sbuf_tensor("st", [128, 512], F32))
    m8a = stack.enter_context(nc.sbuf_tensor("m8a", [128, 16], F32))
    mski = stack.enter_context(nc.sbuf_tensor("mski", [128, 16],
                                              mybir.dt.int32))
    oh8 = stack.enter_context(nc.sbuf_tensor("oh8", [128, 8], F32))
    identb = stack.enter_context(nc.sbuf_tensor("sb_identb", [128, 128], BF16))
    identf = stack.enter_context(nc.sbuf_tensor("sb_identf", [128, 128], F32))
    diagm = stack.enter_context(nc.sbuf_tensor("sb_diagm", [128, 128], F32))
    iota8 = stack.enter_context(nc.sbuf_tensor("sb_iota8", [128, 8], F32))
    onesb = stack.enter_context(nc.sbuf_tensor("sb_onesb", [1, 128], BF16))
    onesf = stack.enter_context(nc.sbuf_tensor("sb_onesf", [1, 128], F32))
    protT = stack.enter_context(nc.sbuf_tensor("sb_protT", [128, 128], F32))
    cons = stack.enter_context(nc.sbuf_tensor("cons", [128, 8 * NQT + 16], F32))
    btrow_b = stack.enter_context(nc.sbuf_tensor("btrow_b", [1, D], BF16))
    bts = stack.enter_context(nc.sbuf_tensor("bts", [32, 128], BF16))
    sc11 = stack.enter_context(nc.sbuf_tensor("sc11", [1, 2], F32))
    lits = stack.enter_context(nc.sbuf_tensor("lits_sb", [128, 2], F32))
    nc.const_aps.register(F32, float(1e-5), lits[:, 0:1])
    nc.const_aps.register(F32, float(-0.5), lits[:, 1:2])

    psA = stack.enter_context(nc.psum_tensor("psA", [128, 512], F32))
    psB = stack.enter_context(nc.psum_tensor("psB", [128, 512], F32))
    psC = stack.enter_context(nc.psum_tensor("psC", [128, 512], F32))
    psE = stack.enter_context(nc.psum_tensor("psE", [128, 512], BF16))
    psF = stack.enter_context(nc.psum_tensor("psF", [128, 512], BF16))
    psG = stack.enter_context(nc.psum_tensor("psG", [128, 128], F32))
    psH = stack.enter_context(nc.psum_tensor("psH", [128, 128], BF16))

    block = stack.enter_context(nc.Block())

    LD = Ctr(stack, nc, "ld")
    SP = Ctr(stack, nc, "sp")
    V = Ctr(stack, nc, "v")
    SC = Ctr(stack, nc, "s")
    T = Ctr(stack, nc, "t")

    TB = {nm: i * NQT for i, nm in enumerate(
        ("z", "hz", "n", "invLk", "Lk2", "minflag", "minidx"))}
    CX = 7 * NQT
    C_S32, C_RS32, C_C20, C_R20, C_DLT, C_NDLT, C_DLTB, C_NDLTB, \
        C_WSP, C_BSP, C_WMX, C_BMX, C_TMPA, C_TMPB = range(CX, CX + 14)

    def tbl(nm, qt0=0, n=NQT):
        return cons[:, TB[nm] + qt0: TB[nm] + qt0 + n]

    def cx(i, n=1):
        return cons[:, i:i + n]

    def stp(i, n=16):
        return st[:, 16 * i: 16 * i + n]

    def p0_sync(sync):
        LD.tick(sync.dma_start(out=identb[:], in_=identb_h[:]), 16)
        LD.tick(sync.dma_start(out=identf[:], in_=identf_h[:]), 16)
        LD.tick(sync.dma_start(out=diagm[:], in_=diagm_h[:]), 16)
        LD.tick(sync.dma_start(out=iota8[:], in_=iota8_h[:]), 16)
        LD.tick(sync.dma_start(out=onesb[:], in_=onesb_h[:]), 16)
        LD.tick(sync.dma_start(out=onesf[:], in_=onesf_h[:]), 16)
        LD.tick(sync.dma_start(out=protT[:], in_=prot_h[:]), 16)
        for nm in TB:
            LD.tick(sync.dma_start(out=tbl(nm), in_=tabs_h[nm + "_t"][:]), 16)
        LD.tick(sync.dma_start(out=lits[:], in_=lits_h[:]), 16)
        LD.tick(sync.dma_start(out=cx(C_S32), in_=s32_h[:]), 16)
        LD.tick(sync.dma_start(out=cx(C_C20), in_=c20_h[:]), 16)
        LD.tick(sync.dma_start(out=AF_[:, 0:S], in_=cost_h[:]), 16)
        LD.tick(sync.dma_start(out=AF_[:, S:2*S], in_=sint_h[:]), 16)
        LD.tick(sync.dma_start(out=st[:, 480:512], in_=fb_h[:]), 16)
        LD.tick(sync.dma_start(out=sc11[0:1, 0:1], in_=wsc_h[:]), 16)
        LD.tick(sync.dma_start(out=sc11[0:1, 1:2], in_=bsc_h[:]), 16)
    block.sync(p0_sync)
    ld_p0 = LD.n

    def p0_vec(v):
        v.wait_ge(LD.h, ld_p0)
        nc.vector.reciprocal(out=cx(C_RS32), in_=cx(C_S32))
        V.tick(nc.vector.reciprocal(out=cx(C_R20), in_=cx(C_C20)))
    block.vector(p0_vec)
    v_p0 = V.n

    COS = AF_[:, 0:S]
    SIN = AF_[:, S:2*S]

    XF = (4096, 8192)
    XNB = (0, 4096)
    STG = (8192, 12288)
    WASTE_B = 16384

    xf_free = {}
    stg_spill = {}
    xnb_used = {}
    p1_last = {}

    for tt in range(NQT):
        buf = tt % 2
        xf = AF_[:, XF[buf]:XF[buf] + D]
        xnb = AB_[:, XNB[buf]:XNB[buf] + D]
        stg = AB_[:, STG[buf]:STG[buf] + D]
        marks = {}

        def p1_sync(sync, tt=tt, xf=xf, marks=marks):
            if tt - 2 in xf_free:
                sync.wait_ge(V.h, xf_free[tt - 2])
            LD.tick(sync.dma_start(out=xf[:], in_=x_h[tt*128:(tt+1)*128, :]), 16)
            marks["x"] = LD.n
        block.sync(p1_sync)

        def p1_sca(s, xf=xf, marks=marks):
            s.wait_ge(LD.h, marks["x"])
            s.activation(out=AB_[:, WASTE_B:WASTE_B+D], in_=xf[:],
                         func=AF.Identity, bias=0.0, scale=1.0,
                         accum_out=st[:, 0:1])
            s.activation(out=AB_[:, WASTE_B:WASTE_B+D], in_=xf[:],
                         func=AF.Square, bias=0.0, scale=1.0,
                         accum_out=st[:, 1:2])
            SC.tick(s.drain())
            marks["sums"] = SC.n
        block.scalar(p1_sca)

        def p1_vec(v, marks=marks):
            v.wait_ge(SC.h, marks["sums"])
            v.tensor_scalar(out=st[:, 2:3], in0=st[:, 0:1],
                            scalar1=float(1.0 / D), scalar2=None, op0=ALU.mult)
            v.tensor_scalar(out=st[:, 3:4], in0=st[:, 1:2],
                            scalar1=float(1.0 / D), scalar2=None, op0=ALU.mult)
            v.drain()
            v.tensor_tensor(out=st[:, 4:5], in0=st[:, 2:3], in1=st[:, 2:3],
                            op=ALU.mult)
            v.drain()
            v.tensor_tensor(out=st[:, 5:6], in0=st[:, 3:4],
                            in1=st[:, 4:5], op=ALU.subtract)
            V.tick(v.drain())
            marks["var"] = V.n
        block.vector(p1_vec)

        def p1_sca2(s, marks=marks):
            s.wait_ge(V.h, marks["var"])
            s.activation(out=st[:, 6:7], in_=st[:, 5:6], func=AF.Ln,
                         bias=float(1e-5), scale=1.0)
            s.drain()
            s.activation(out=st[:, 7:8], in_=st[:, 6:7], func=AF.Exp,
                         bias=0.0, scale=-0.5)
            SC.tick(s.drain())
            marks["rstd"] = SC.n
        block.scalar(p1_sca2)

        def p1_vec2(v, tt=tt, xf=xf, xnb=xnb, marks=marks):
            v.wait_ge(SC.h, marks["rstd"])
            if tt - 2 in xnb_used:
                v.wait_ge(T.h, xnb_used[tt - 2])
            V.tick(v.tensor_scalar(out=xnb[:], in0=xf[:], scalar1=st[:, 2:3],
                                   scalar2=st[:, 7:8], op0=ALU.subtract,
                                   op1=ALU.mult))
            marks["xn"] = V.n
            xf_free[tt] = V.n
        block.vector(p1_vec2)

        for g in range(8):
            ps = psE if g % 2 == 0 else psF

            def p1_ten(t, tt=tt, g=g, ps=ps, xnb=xnb, marks=marks):
                t.wait_ge(V.h, marks["xn"])
                key = ("ev", g - 2) if g >= 2 else None
                if key and key in marks:
                    t.wait_ge(V.h, marks[key])
                elif g < 2 and tt > 0:
                    t.wait_ge(V.h, p1_last.get(("ev", tt - 1, 6 + g), 0))
                ins = None
                for u in range(4):
                    c = g * 4 + u
                    ins = t.transpose(out=ps[:, u*128:(u+1)*128],
                                      in_=xnb[:, c*128:(c+1)*128],
                                      identity=identb[:])
                T.tick(ins)
                marks[("tr", g)] = T.n
            block.tensor(p1_ten)

            def p1_vec3(v, tt=tt, g=g, ps=ps, stg=stg, marks=marks):
                v.wait_ge(T.h, marks[("tr", g)])
                if tt - 2 in stg_spill and g == 0:
                    v.wait_ge(SP.h, stg_spill[tt - 2])
                V.tick(v.tensor_copy(out=stg[:, g*512:(g+1)*512], in_=ps[:]))
                marks[("ev", g)] = V.n
                p1_last[("ev", tt, g)] = V.n
            block.vector(p1_vec3)

        def p1_gp(g_, tt=tt, stg=stg, marks=marks):
            g_.wait_ge(V.h, marks[("ev", 7)])
            for part in range(4):
                dst = xnsp_h[part*8:(part+1)*8, :,
                             tt*128:(tt+1)*128].rearrange("c p t -> p c t")
                srcp = stg[:, part*1024:(part+1)*1024].rearrange(
                    "p (c t) -> p c t", c=8)
                SP.tick(g_.dma_start(out=dst, in_=srcp), 16)
            stg_spill[tt] = SP.n
        block.gpsimd(p1_gp)

    WST = (4096, 8192)
    QSC = 12288
    RT1 = 12544
    RT2 = 12800
    WPB = 0
    XNS = 16384
    KRO = 24576
    VTT = 40960
    VTMP = 57600

    jobs = [("k", i) for i in range(HKV)] + [("v", i) for i in range(HKV)] + \
           [("q", i) for i in range(H)]
    wsrc = {"k": wkt_h, "v": wvt_h, "q": wqt_h}

    VTW = 129
    ps_evac = {}
    wst_free = {}
    wpb_free = {}
    xns_free = 0
    qst_spill = {}

    last_p1_sp = SP.n

    def p2_ones(v):
        V.tick(v.memset(AB_[:, VTT:VTT + HKV*16*VTW], 1.0))
    block.vector(p2_ones)

    v_after_p1 = V.n
    t_after_p1 = T.n

    for gi in range(0, len(jobs), 4):
        grp = jobs[gi:gi + 4]
        for j, (kind, idx) in enumerate(grp):
            sbuf = (gi // 4 * 4 + j) % 2
            wst = AF_[:, WST[sbuf]:WST[sbuf] + D]
            marks = {}

            def p2_syncw(sync, kind=kind, idx=idx, wst=wst, sbuf=sbuf,
                         marks=marks, gi=gi, j=j):
                if gi == 0 and j == 0:
                    sync.wait_ge(V.h, v_after_p1)
                key = ("wst", sbuf)
                if key in wst_free:
                    sync.wait_ge(V.h, wst_free[key])
                src = wsrc[kind][:, idx*128:(idx+1)*128].rearrange(
                    "(c p) o -> p c o", p=128)
                LD.tick(sync.dma_start(
                    out=wst.rearrange("p (c o) -> p c o", c=32),
                    in_=src), 16)
                marks["w"] = LD.n
            block.sync(p2_syncw)

            def p2_castw(v, wst=wst, sbuf=sbuf, j=j, marks=marks):
                v.wait_ge(LD.h, marks["w"])
                key = ("wpb", j)
                if key in wpb_free:
                    v.wait_ge(T.h, wpb_free[key])
                V.tick(v.tensor_copy(out=AB_[:, WPB + j*D:WPB + (j+1)*D],
                                     in_=wst[:]))
                wst_free[("wst", sbuf)] = V.n
                marks["cast"] = V.n
                wpb_free[("wpb", j)] = None
            block.vector(p2_castw)
            grp[j] = (kind, idx, marks)

        for tt8 in range(8):
            xmarks = {}

            def p2_syncx(sync, tt8=tt8, xmarks=xmarks):
                sync.wait_ge(T.h, xns_free)
                src = xnsp_h[:, :, tt8*256:(tt8+1)*256].rearrange(
                    "c p t -> p c t")
                sync.wait_ge(SP.h, last_p1_sp)
                LD.tick(sync.dma_start(out=AB_[:, XNS:XNS + 32*256],
                                       in_=src), 16)
                xmarks["xn"] = LD.n
            block.sync(p2_syncx)

            for j, (kind, idx, wmarks) in enumerate(grp):
                ps = psA if (tt8 * 4 + j) % 2 == 0 else psB
                pkey = id(ps)
                jm = {}

                def p2_mm(t, j=j, ps=ps, pkey=pkey, xmarks=xmarks,
                          wmarks=wmarks, jm=jm):
                    t.wait_ge(LD.h, xmarks["xn"])
                    t.wait_ge(V.h, wmarks["cast"])
                    if pkey in ps_evac:
                        t.wait_ge(V.h, ps_evac[pkey])
                    ins = None
                    for c in range(32):
                        ins = t.matmul(
                            out=ps[:, 0:256],
                            lhsT=AB_[:, WPB + j*D + c*128:WPB + j*D + (c+1)*128],
                            rhs=AB_[:, XNS + c*256:XNS + (c+1)*256],
                            start=(c == 0), stop=(c == 31))
                    T.tick(ins)
                    jm["mm"] = T.n
                    wpb_free[("wpb", j)] = T.n
                block.tensor(p2_mm)

                if kind == "k":
                    def p2_kevac(v, ps=ps, pkey=pkey, jm=jm):
                        v.wait_ge(T.h, jm["mm"])
                        if "psC" in ps_evac:
                            v.wait_ge(T.h, ps_evac["psC_mm"])
                        V.tick(v.tensor_copy(out=AF_[:, QSC:QSC+256],
                                             in_=ps[:, 0:256]))
                        ps_evac[pkey] = V.n
                        jm["ev"] = V.n
                    block.vector(p2_kevac)

                    def p2_krot(t, jm=jm):
                        t.wait_ge(V.h, jm["ev"])
                        if "psC" in ps_evac:
                            t.wait_ge(V.h, ps_evac["psC"])
                        T.tick(t.matmul(out=psC[:, 0:256], lhsT=protT[:],
                                        rhs=AF_[:, QSC:QSC+256],
                                        start=True, stop=True))
                        jm["rot"] = T.n
                        ps_evac["psC_mm"] = T.n
                    block.tensor(p2_krot)

                    def p2_krope(v, idx=idx, tt8=tt8, jm=jm):
                        v.wait_ge(T.h, jm["rot"])
                        t1 = AF_[:, RT1:RT1+256]
                        t2 = AF_[:, RT2:RT2+256]
                        cs = COS[:, tt8*256:(tt8+1)*256]
                        sn = SIN[:, tt8*256:(tt8+1)*256]
                        v.tensor_tensor(out=t1, in0=AF_[:, QSC:QSC+256],
                                        in1=cs, op=ALU.mult)
                        v.tensor_tensor(out=t2, in0=psC[:, 0:256], in1=sn,
                                        op=ALU.mult)
                        dst = AB_[:, KRO + idx*S + tt8*256:
                                  KRO + idx*S + (tt8+1)*256]
                        V.tick(v.tensor_tensor(out=dst, in0=t1, in1=t2,
                                               op=ALU.add))
                        ps_evac["psC"] = V.n
                    block.vector(p2_krope)
                elif kind == "q":
                    def p2_qevac(v, ps=ps, pkey=pkey, jm=jm):
                        v.wait_ge(T.h, jm["mm"])
                        if "psC" in ps_evac:
                            v.wait_ge(T.h, ps_evac["psC_mm"])
                        V.tick(v.tensor_scalar(out=AF_[:, QSC:QSC+256],
                                               in0=ps[:, 0:256],
                                               scalar1=cx(C_RS32),
                                               scalar2=None, op0=ALU.mult))
                        ps_evac[pkey] = V.n
                        jm["ev"] = V.n
                    block.vector(p2_qevac)

                    def p2_qrot(t, jm=jm):
                        t.wait_ge(V.h, jm["ev"])
                        if "psC" in ps_evac:
                            t.wait_ge(V.h, ps_evac["psC"])
                        T.tick(t.matmul(out=psC[:, 0:256], lhsT=protT[:],
                                        rhs=AF_[:, QSC:QSC+256],
                                        start=True, stop=True))
                        jm["rot"] = T.n
                        ps_evac["psC_mm"] = T.n
                    block.tensor(p2_qrot)

                    def p2_qrope(v, idx=idx, tt8=tt8, jm=jm):
                        v.wait_ge(T.h, jm["rot"])
                        t1 = AF_[:, RT1:RT1+256]
                        t2 = AF_[:, RT2:RT2+256]
                        cs = COS[:, tt8*256:(tt8+1)*256]
                        sn = SIN[:, tt8*256:(tt8+1)*256]
                        v.tensor_tensor(out=t1, in0=AF_[:, QSC:QSC+256],
                                        in1=cs, op=ALU.mult)
                        v.tensor_tensor(out=t2, in0=psC[:, 0:256], in1=sn,
                                        op=ALU.mult)
                        dst = AB_[:, VTMP + 256:VTMP + 512]
                        V.tick(v.tensor_tensor(out=dst, in0=t1, in1=t2,
                                               op=ALU.add))
                        ps_evac["psC"] = V.n
                        jm["qr"] = V.n
                    block.vector(p2_qrope)

                    def p2_qspill(g_, idx=idx, tt8=tt8, jm=jm):
                        g_.wait_ge(V.h, jm["qr"])
                        SP.tick(g_.dma_start(
                            out=qsp_h[idx, :, tt8*256:(tt8+1)*256],
                            in_=AB_[:, VTMP + 256:VTMP + 512]), 16)
                        qst_spill[(idx, tt8)] = SP.n
                    block.gpsimd(p2_qspill)
                    def p2_qwait(v, key=(idx, tt8)):
                        v.wait_ge(SP.h, qst_spill[key])
                    block.vector(p2_qwait)
                else:
                    def p2_vevac(v, ps=ps, pkey=pkey, jm=jm):
                        v.wait_ge(T.h, jm["mm"])
                        V.tick(v.tensor_copy(out=AB_[:, VTMP:VTMP+256],
                                             in_=ps[:, 0:256]))
                        ps_evac[pkey] = V.n
                        jm["ev"] = V.n
                    block.vector(p2_vevac)

                    def p2_vtr(t, jm=jm):
                        t.wait_ge(V.h, jm["ev"])
                        t.transpose(out=psE[:, 0:128],
                                    in_=AB_[:, VTMP:VTMP+128],
                                    identity=identb[:])
                        T.tick(t.transpose(out=psE[:, 128:256],
                                           in_=AB_[:, VTMP+128:VTMP+256],
                                           identity=identb[:]))
                        jm["tr"] = T.n
                    block.tensor(p2_vtr)

                    def p2_vev2(v, idx=idx, tt8=tt8, jm=jm):
                        v.wait_ge(T.h, jm["tr"])
                        for sub in range(2):
                            kt = tt8 * 2 + sub
                            dst = AB_[:, VTT + (idx*16 + kt)*VTW:
                                      VTT + (idx*16 + kt)*VTW + 128]
                            ins = v.tensor_copy(out=dst,
                                                in_=psE[:, sub*128:(sub+1)*128])
                        V.tick(ins)
                        jm["vt"] = V.n
                    block.vector(p2_vev2)

            xns_free = T.n

    SSB = 0
    MMF = 10368
    U2A = 12416
    U2B = 12672
    OHW = 12928
    QH = (0, 2048)
    PB = 4096
    PT = 6144
    CW = 8192
    ATH = (10240, 12288)
    ATQ = 14336
    CW2 = 14464

    S_ACC4, S_MUS, S_S2, S_MU, S_VAR, S_ISG, S_SG, S_LAM, S_L25, S_L4, \
        S_LO, S_CLO, S_HI, S_CHI, S_TC, S_CC, S_LNC, S_TP, S_LNP, S_T1, \
        S_T2, S_NTC, S_M45, S_J, S_IDX, S_TAU, S_NTAU, S_Z, S_RZ = range(1, 30)

    qh_free = {}
    ath_free = {}
    p4_ps_free = {}

    v_after_p23 = V.n
    t_after_p23 = T.n

    def qoff(half, i):
        qt0 = HALVES[half][0]
        off = 0
        for k in range(i):
            off += (qt0 + k + 1) * 128
        return off

    for kvh in range(HKV):
        for hh in range(4):
            h = kvh * 4 + hh
            hbuf = h % 2
            qh = AB_[:, QH[hbuf]:QH[hbuf] + S]
            ath = AB_[:, ATH[hbuf]:ATH[hbuf] + S]
            hm = {}

            def p4_syncq(sync, h=h, hbuf=hbuf, qh=qh, hm=hm):
                if ("qh", hbuf) in qh_free:
                    sync.wait_ge(T.h, qh_free[("qh", hbuf)])
                else:
                    sync.wait_ge(T.h, t_after_p23)
                sync.wait_ge(SP.h, qst_spill[(h, 7)])
                LD.tick(sync.dma_start(out=qh[:], in_=qsp_h[h]), 16)
                hm["q"] = LD.n
            block.sync(p4_syncq)

            for half in range(2):
                qt0, nh = HALVES[half]
                tiles = list(range(qt0, qt0 + nh))

                for i, qt in enumerate(tiles):
                    Lk = (qt + 1) * 128
                    base = SSB + qoff(half, i)
                    tm = {}
                    tm["acc4_wait"] = p4_ps_free.get("acc4")
                    nkc = (Lk + 511) // 512
                    for kc in range(nkc):
                        w = min(512, Lk - kc * 512)
                        ps = psA if kc % 2 == 0 else psB
                        pk = ("p4", id(ps))

                        def p4_mm(t, qt=qt, kc=kc, w=w, ps=ps, pk=pk, qh=qh,
                                  kvh=kvh, hm=hm, tm=tm):
                            t.wait_ge(LD.h, hm["q"])
                            if pk in p4_ps_free:
                                t.wait_ge(SC.h, p4_ps_free[pk])
                            else:
                                t.wait_ge(V.h, v_after_p23)
                            T.tick(t.matmul(
                                out=ps[:, 0:w],
                                lhsT=qh[:, qt*128:(qt+1)*128],
                                rhs=AB_[:, KRO + kvh*S + kc*512:
                                        KRO + kvh*S + kc*512 + w],
                                start=True, stop=True))
                            tm[("mm", kc)] = T.n
                        block.tensor(p4_mm)

                        def p4_ev(s, i=i, kc=kc, w=w, ps=ps, pk=pk, base=base,
                                  tm=tm):
                            s.wait_ge(T.h, tm[("mm", kc)])
                            if kc == 0 and tm["acc4_wait"] is not None:
                                s.wait_ge(V.h, tm["acc4_wait"])
                            SC.tick(s.activation(
                                out=AF_[:, base + kc*512: base + kc*512 + w],
                                in_=ps[:, 0:w], func=AF.Identity,
                                bias=0.0, scale=1.0,
                                accum_out=stp(S_ACC4)[:, kc:kc+1]))
                            p4_ps_free[pk] = SC.n
                            tm[("ev", kc)] = SC.n
                        block.scalar(p4_ev)

                    def p4_sq(s, i=i, base=base, Lk=Lk, tm=tm, nkc=nkc):
                        s.activation(
                            out=AB_[:, CW2:CW2+Lk],
                            in_=AF_[:, base:base+Lk], func=AF.Square,
                            bias=0.0, scale=1.0,
                            accum_out=stp(S_S2)[:, i:i+1])
                        SC.tick(s.drain())
                        tm["sq"] = SC.n
                    block.scalar(p4_sq)

                    def p4_stats(v, i=i, nkc=nkc, tm=tm):
                        v.wait_ge(SC.h, tm["sq"])
                        V.tick(v.tensor_reduce(
                            out=stp(S_MUS)[:, i:i+1],
                            in_=stp(S_ACC4)[:, 0:nkc], axis=AX.X, op=ALU.add))
                        tm["mus"] = V.n
                        p4_ps_free["acc4"] = V.n
                    block.vector(p4_stats)

                    def p4_diag(v, qt=qt, base=base, tm=tm):
                        dst = AF_[:, base + qt*128: base + (qt+1)*128]
                        V.tick(v.tensor_tensor(out=dst, in0=dst, in1=diagm[:],
                                               op=ALU.add))
                        tm["diag"] = V.n
                    block.vector(p4_diag)

                sel_mark = {}

                def p4_init_v(v, half=half, qt0=qt0, nh=nh, sel_mark=sel_mark):
                    nhs = slice(0, nh)
                    v.tensor_tensor(out=stp(S_MU)[:, nhs],
                                    in0=stp(S_MUS)[:, nhs],
                                    in1=tbl("invLk", qt0, nh), op=ALU.mult)
                    v.tensor_tensor(out=stp(S_T1)[:, nhs],
                                    in0=stp(S_S2)[:, nhs],
                                    in1=tbl("invLk", qt0, nh), op=ALU.mult)
                    v.drain()
                    v.tensor_tensor(out=stp(S_T2)[:, nhs],
                                    in0=stp(S_MU)[:, nhs],
                                    in1=stp(S_MU)[:, nhs], op=ALU.mult)
                    v.drain()
                    v.tensor_tensor(out=stp(S_VAR)[:, nhs],
                                    in0=stp(S_T1)[:, nhs],
                                    in1=stp(S_T2)[:, nhs], op=ALU.subtract)
                    v.drain()
                    v.tensor_scalar(out=stp(S_VAR)[:, nhs],
                                    in0=stp(S_VAR)[:, nhs],
                                    scalar1=float(1e-12), scalar2=None,
                                    op0=ALU.max)
                    V.tick(v.drain())
                    sel_mark["var"] = V.n
                block.vector(p4_init_v)

                def p4_init_s(s, nh=nh, sel_mark=sel_mark):
                    s.wait_ge(V.h, sel_mark["var"])
                    nhs = slice(0, nh)
                    s.activation(out=stp(S_T1)[:, nhs], in_=stp(S_VAR)[:, nhs],
                                 func=AF.Ln, bias=0.0, scale=1.0)
                    s.drain()
                    s.activation(out=stp(S_ISG)[:, nhs],
                                 in_=stp(S_T1)[:, nhs],
                                 func=AF.Exp, bias=0.0, scale=-0.5)
                    SC.tick(s.drain())
                    sel_mark["isg"] = SC.n
                block.scalar(p4_init_s)

                def p4_init_v2(v, qt0=qt0, nh=nh, sel_mark=sel_mark):
                    v.wait_ge(SC.h, sel_mark["isg"])
                    nhs = slice(0, nh)
                    v.tensor_tensor(out=stp(S_SG)[:, nhs],
                                    in0=stp(S_VAR)[:, nhs],
                                    in1=stp(S_ISG)[:, nhs], op=ALU.mult)
                    v.tensor_tensor(out=stp(S_LAM)[:, nhs],
                                    in0=tbl("hz", qt0, nh),
                                    in1=stp(S_ISG)[:, nhs], op=ALU.mult)
                    v.tensor_copy(out=stp(S_CLO)[:, nhs],
                                  in_=tbl("n", qt0, nh))
                    v.memset(stp(S_CHI)[:, nhs], 0.0)
                    v.drain()
                    v.scalar_tensor_tensor(out=stp(S_LO)[:, nhs],
                                           in0=stp(S_SG)[:, nhs], scalar=-25.0,
                                           in1=stp(S_MU)[:, nhs],
                                           op0=ALU.mult, op1=ALU.add)
                    v.scalar_tensor_tensor(out=stp(S_HI)[:, nhs],
                                           in0=stp(S_SG)[:, nhs], scalar=25.0,
                                           in1=stp(S_MU)[:, nhs],
                                           op0=ALU.mult, op1=ALU.add)
                    v.scalar_tensor_tensor(out=stp(S_M45)[:, nhs],
                                           in0=stp(S_SG)[:, nhs], scalar=-8.0,
                                           in1=stp(S_MU)[:, nhs],
                                           op0=ALU.mult, op1=ALU.add)
                    v.scalar_tensor_tensor(out=stp(S_T2)[:, nhs],
                                           in0=stp(S_SG)[:, nhs], scalar=4.0,
                                           in1=stp(S_MU)[:, nhs],
                                           op0=ALU.mult, op1=ALU.add)
                    v.tensor_scalar(out=stp(S_L25)[:, nhs],
                                    in0=stp(S_LAM)[:, nhs], scalar1=0.25,
                                    scalar2=None, op0=ALU.mult)
                    v.tensor_scalar(out=stp(S_L4)[:, nhs],
                                    in0=stp(S_LAM)[:, nhs], scalar1=4.0,
                                    scalar2=None, op0=ALU.mult)
                    v.drain()
                    v.tensor_scalar(out=stp(S_LO)[:, nhs],
                                    in0=stp(S_LO)[:, nhs], scalar1=-1.0,
                                    scalar2=None, op0=ALU.add)
                    v.tensor_scalar(out=stp(S_HI)[:, nhs],
                                    in0=stp(S_HI)[:, nhs], scalar1=1.0,
                                    scalar2=None, op0=ALU.add)
                    v.tensor_scalar(out=stp(S_T2)[:, nhs],
                                    in0=stp(S_T2)[:, nhs], scalar1=-70.0,
                                    scalar2=None, op0=ALU.add)
                    v.drain()
                    v.tensor_tensor(out=stp(S_M45)[:, nhs],
                                    in0=stp(S_M45)[:, nhs],
                                    in1=stp(S_T2)[:, nhs], op=ALU.max)
                    v.tensor_tensor(out=stp(S_T1)[:, nhs],
                                    in0=stp(S_SG)[:, nhs],
                                    in1=tbl("z", qt0, nh), op=ALU.mult)
                    v.tensor_scalar(out=stp(S_T2)[:, nhs],
                                    in0=stp(S_LO)[:, nhs], scalar1=1e-5,
                                    scalar2=None, op0=ALU.add)
                    v.tensor_scalar(out=stp(S_LNP)[:, nhs],
                                    in0=stp(S_HI)[:, nhs], scalar1=-1e-5,
                                    scalar2=None, op0=ALU.add)
                    v.drain()
                    v.tensor_tensor(out=stp(S_TC)[:, nhs],
                                    in0=stp(S_MU)[:, nhs],
                                    in1=stp(S_T1)[:, nhs], op=ALU.add)
                    v.drain()
                    v.tensor_tensor(out=stp(S_TC)[:, nhs],
                                    in0=stp(S_TC)[:, nhs],
                                    in1=stp(S_T2)[:, nhs], op=ALU.max)
                    v.drain()
                    v.tensor_tensor(out=stp(S_TC)[:, nhs],
                                    in0=stp(S_TC)[:, nhs],
                                    in1=stp(S_LNP)[:, nhs], op=ALU.min)
                    v.drain()
                    v.tensor_scalar(out=stp(S_NTC)[:, nhs],
                                    in0=stp(S_TC)[:, nhs], scalar1=-1.0,
                                    scalar2=None, op0=ALU.mult)
                    V.tick(v.drain())
                    sel_mark["t0"] = V.n
                block.vector(p4_init_v2)

                for r in range(R_ROUNDS):
                    lntgt = LNTGT1 if r < R_ROUNDS - 2 else LNTGT2

                    def p4_cnt_s(s, half=half, nh=nh, tiles=tiles, r=r,
                                 sel_mark=sel_mark):
                        s.wait_ge(V.h, sel_mark["t0" if r == 0 else ("t", r)])
                        for i, qt in enumerate(tiles):
                            if i % 2 == 0:
                                continue
                            Lk = (qt + 1) * 128
                            base = SSB + qoff(half, i)
                            ins = s.activation(out=AB_[:, CW2:CW2+Lk],
                                         in_=AF_[:, base:base+Lk],
                                         func=AF.Sign,
                                         bias=stp(S_NTC)[:, i:i+1], scale=1.0,
                                         accum_out=stp(S_T2)[:, i:i+1])
                        s.drain()
                        SC.tick(s.drain())
                        sel_mark[("cs", r)] = SC.n
                    block.scalar(p4_cnt_s)

                    def p4_cnt_v(v, half=half, qt0=qt0, nh=nh, tiles=tiles,
                                 r=r, lntgt=lntgt, sel_mark=sel_mark):
                        for i, qt in enumerate(tiles):
                            if i % 2 == 1:
                                continue
                            Lk = (qt + 1) * 128
                            base = SSB + qoff(half, i)
                            v.tensor_scalar(out=AB_[:, CW:CW+Lk],
                                            in0=AF_[:, base:base+Lk],
                                            scalar1=stp(S_TC)[:, i:i+1],
                                            scalar2=0.0, op0=ALU.is_ge,
                                            op1=ALU.add,
                                            accum_out=stp(S_CC)[:, i:i+1])
                        v.wait_ge(SC.h, sel_mark[("cs", r)])
                        v.drain()
                        nhs = slice(0, nh)
                        for i, qt in enumerate(tiles):
                            if i % 2 == 0:
                                continue
                            v.scalar_tensor_tensor(
                                out=stp(S_CC)[:, i:i+1],
                                in0=stp(S_T2)[:, i:i+1], scalar=0.5,
                                in1=tbl("Lk2", qt0 + i, 1),
                                op0=ALU.mult, op1=ALU.add)
                        v.drain()
                        v.tensor_scalar(out=mski[:, nhs],
                                        in0=stp(S_CC)[:, nhs],
                                        scalar1=float(KEEP), scalar2=None,
                                        op0=ALU.is_ge)
                        v.drain()
                        nc.vector.select(out=stp(S_LO)[:, nhs],
                                         mask=mski[:, nhs],
                                         on_true=stp(S_TC)[:, nhs],
                                         on_false=stp(S_LO)[:, nhs])
                        nc.vector.select(out=stp(S_CLO)[:, nhs],
                                         mask=mski[:, nhs],
                                         on_true=stp(S_CC)[:, nhs],
                                         on_false=stp(S_CLO)[:, nhs])
                        nc.vector.select(out=stp(S_HI)[:, nhs],
                                         mask=mski[:, nhs],
                                         on_true=stp(S_HI)[:, nhs],
                                         on_false=stp(S_TC)[:, nhs])
                        nc.vector.select(out=stp(S_CHI)[:, nhs],
                                         mask=mski[:, nhs],
                                         on_true=stp(S_CHI)[:, nhs],
                                         on_false=stp(S_CC)[:, nhs])
                        v.drain()
                        v.tensor_scalar(out=stp(S_T2)[:, nhs],
                                        in0=stp(S_CC)[:, nhs],
                                        scalar1=0.5, scalar2=None,
                                        op0=ALU.max)
                        V.tick(v.drain())
                        sel_mark[("upd", r)] = V.n
                    block.vector(p4_cnt_v)

                    if r == R_ROUNDS - 1:
                        break

                    def p4_ln_s(s, nh=nh, r=r, sel_mark=sel_mark):
                        s.wait_ge(V.h, sel_mark[("upd", r)])
                        s.activation(out=stp(S_LNC)[:, 0:nh],
                                     in_=stp(S_T2)[:, 0:nh],
                                     func=AF.Ln, bias=0.0, scale=1.0)
                        SC.tick(s.drain())
                        sel_mark[("ln", r)] = SC.n
                    block.scalar(p4_ln_s)

                    def p4_next_v(v, nh=nh, r=r, lntgt=lntgt, sel_mark=sel_mark):
                        v.wait_ge(SC.h, sel_mark[("ln", r)])
                        v.drain()
                        nhs = slice(0, nh)
                        v.tensor_tensor(out=stp(S_T1)[:, nhs],
                                        in0=stp(S_TC)[:, nhs],
                                        in1=stp(S_TP)[:, nhs],
                                        op=ALU.subtract)
                        v.tensor_tensor(out=stp(S_IDX)[:, nhs],
                                        in0=stp(S_LNP)[:, nhs],
                                        in1=stp(S_LNC)[:, nhs],
                                        op=ALU.subtract)
                        v.tensor_scalar(out=stp(S_J)[:, nhs],
                                        in0=stp(S_LNC)[:, nhs],
                                        scalar1=float(-lntgt), scalar2=None,
                                        op0=ALU.add)
                        v.tensor_copy(out=stp(S_TP)[:, nhs],
                                      in_=stp(S_TC)[:, nhs])
                        v.tensor_copy(out=stp(S_LNP)[:, nhs],
                                      in_=stp(S_LNC)[:, nhs])
                        v.tensor_scalar(out=stp(S_TAU)[:, nhs],
                                        in0=stp(S_LO)[:, nhs], scalar1=1e-5,
                                        scalar2=None, op0=ALU.add)
                        v.tensor_scalar(out=stp(S_NTAU)[:, nhs],
                                        in0=stp(S_HI)[:, nhs], scalar1=-1e-5,
                                        scalar2=None, op0=ALU.add)
                        v.drain()
                        if r >= 1:
                            nc.vector.reciprocal(out=stp(S_T2)[:, nhs],
                                                 in_=stp(S_T1)[:, nhs])
                            v.drain()
                            v.tensor_tensor(out=stp(S_T1)[:, nhs],
                                            in0=stp(S_IDX)[:, nhs],
                                            in1=stp(S_T2)[:, nhs], op=ALU.mult)
                            v.drain()
                            v.tensor_tensor(out=stp(S_T2)[:, nhs],
                                            in0=stp(S_T1)[:, nhs],
                                            in1=stp(S_L4)[:, nhs], op=ALU.min)
                            v.tensor_scalar(out=mski[:, nhs],
                                            in0=stp(S_T1)[:, nhs], scalar1=0.0,
                                            scalar2=None, op0=ALU.is_gt)
                            v.drain()
                            v.tensor_tensor(out=stp(S_T2)[:, nhs],
                                            in0=stp(S_T2)[:, nhs],
                                            in1=stp(S_L25)[:, nhs], op=ALU.max)
                            v.drain()
                            nc.vector.select(out=stp(S_T2)[:, nhs],
                                             mask=mski[:, nhs],
                                             on_true=stp(S_T2)[:, nhs],
                                             on_false=stp(S_LAM)[:, nhs])
                            v.drain()
                        else:
                            v.tensor_copy(out=stp(S_T2)[:, nhs],
                                          in_=stp(S_LAM)[:, nhs])
                            v.drain()
                        nc.vector.reciprocal(out=stp(S_T2)[:, nhs],
                                             in_=stp(S_T2)[:, nhs])
                        v.drain()
                        v.tensor_tensor(out=stp(S_T1)[:, nhs],
                                        in0=stp(S_J)[:, nhs],
                                        in1=stp(S_T2)[:, nhs], op=ALU.mult)
                        v.drain()
                        v.tensor_tensor(out=stp(S_TC)[:, nhs],
                                        in0=stp(S_TC)[:, nhs],
                                        in1=stp(S_T1)[:, nhs], op=ALU.add)
                        v.drain()
                        v.tensor_tensor(out=stp(S_TC)[:, nhs],
                                        in0=stp(S_TC)[:, nhs],
                                        in1=stp(S_TAU)[:, nhs], op=ALU.max)
                        v.drain()
                        v.tensor_tensor(out=stp(S_TC)[:, nhs],
                                        in0=stp(S_TC)[:, nhs],
                                        in1=stp(S_NTAU)[:, nhs], op=ALU.min)
                        v.drain()
                        v.tensor_scalar(out=stp(S_NTC)[:, nhs],
                                        in0=stp(S_TC)[:, nhs],
                                        scalar1=-1.0, scalar2=None,
                                        op0=ALU.mult)
                        V.tick(v.drain())
                        sel_mark[("t", r + 1)] = V.n
                        if r == 0:
                            pass
                    block.vector(p4_next_v)

                for i, qt in enumerate(tiles):
                    Lk = (qt + 1) * 128
                    base = SSB + qoff(half, i)
                    fm = {}

                    def p4_fin_v(v, i=i, qt=qt, Lk=Lk, base=base, half=half,
                                 sel_mark=sel_mark, fm=fm):
                        v.wait_ge(V.h, sel_mark[("upd", R_ROUNDS - 1)])
                        mm = AF_[:, MMF:MMF+Lk]
                        v.tensor_scalar(out=AB_[:, CW:CW+Lk],
                                        in0=AF_[:, base:base+Lk],
                                        scalar1=stp(S_HI)[:, i:i+1],
                                        scalar2=0.0, op0=ALU.is_gt,
                                        op1=ALU.add,
                                        accum_out=stp(S_J)[:, i:i+1])
                        v.tensor_scalar(out=mm, in0=AF_[:, base:base+Lk],
                                        scalar1=stp(S_HI)[:, i:i+1],
                                        scalar2=float(-MASKC), op0=ALU.is_gt,
                                        op1=ALU.mult)
                        v.tensor_tensor(out=mm, in0=AF_[:, base:base+Lk],
                                        in1=mm, op=ALU.add)
                        nc.vector.max(out=m8a[:, 0:8], in_=mm)
                        v.drain()
                        v.tensor_scalar(out=stp(S_IDX)[:, i:i+1],
                                        in0=stp(S_J)[:, i:i+1],
                                        scalar1=-1.0, scalar2=float(KEEP - 1),
                                        op0=ALU.mult, op1=ALU.add)
                        v.drain()
                        v.tensor_scalar(out=stp(S_IDX)[:, i:i+1],
                                        in0=stp(S_IDX)[:, i:i+1],
                                        scalar1=7.0, scalar2=0.0,
                                        op0=ALU.min, op1=ALU.max)
                        v.drain()
                        v.scalar_tensor_tensor(
                            out=AF_[:, OHW:OHW+8],
                            in0=iota8[:], scalar=stp(S_IDX)[:, i:i+1],
                            in1=m8a[:, 0:8], op0=ALU.is_equal, op1=ALU.mult)
                        v.drain()
                        v.tensor_reduce(out=stp(S_TAU)[:, i:i+1],
                                        in_=AF_[:, OHW:OHW+8], axis=AX.X,
                                        op=ALU.add)
                        v.drain()
                        if qt == 1:
                            sb = AF_[:, base:base+Lk]
                            r01 = AF_[:, U2A:U2A+Lk]
                            u2 = AF_[:, U2B:U2B+Lk]
                            v.tensor_scalar(out=r01, in0=sb, scalar1=-1e29,
                                            scalar2=None, op0=ALU.is_gt)
                            v.tensor_tensor(out=u2, in0=r01, in1=sb,
                                            op=ALU.mult)
                            v.tensor_scalar(out=u2, in0=u2, scalar1=-1.0,
                                            scalar2=None, op0=ALU.mult)
                            v.tensor_scalar(out=r01, in0=r01, scalar1=0.5,
                                            scalar2=float(-MASKC),
                                            op0=ALU.is_lt, op1=ALU.mult)
                            v.tensor_tensor(out=u2, in0=u2, in1=r01,
                                            op=ALU.add)
                            nc.vector.max(out=m8a[:, 8:16], in_=u2)
                            v.drain()
                            v.scalar_tensor_tensor(
                                out=AF_[:, OHW:OHW+8],
                                in0=iota8[:], scalar=tbl("minidx", qt, 1),
                                in1=m8a[:, 8:16], op0=ALU.is_equal,
                                op1=ALU.mult)
                            v.drain()
                            v.tensor_reduce(out=stp(S_T1)[:, i:i+1],
                                            in_=AF_[:, OHW:OHW+8], axis=AX.X,
                                            op=ALU.add)
                            v.drain()
                            v.tensor_scalar(out=stp(S_T1)[:, i:i+1],
                                            in0=stp(S_T1)[:, i:i+1],
                                            scalar1=-1.0, scalar2=None,
                                            op0=ALU.mult)
                            v.drain()
                            v.tensor_tensor(out=stp(S_T2)[:, i:i+1],
                                            in0=stp(S_T1)[:, i:i+1],
                                            in1=stp(S_TAU)[:, i:i+1],
                                            op=ALU.subtract)
                            v.drain()
                            v.tensor_tensor(out=stp(S_T2)[:, i:i+1],
                                            in0=stp(S_T2)[:, i:i+1],
                                            in1=tbl("minflag", qt, 1),
                                            op=ALU.mult)
                            v.drain()
                            v.tensor_tensor(out=stp(S_TAU)[:, i:i+1],
                                            in0=stp(S_TAU)[:, i:i+1],
                                            in1=stp(S_T2)[:, i:i+1],
                                            op=ALU.add)
                            v.drain()
                        v.tensor_tensor(out=stp(S_TAU)[:, i:i+1],
                                        in0=stp(S_TAU)[:, i:i+1],
                                        in1=stp(S_M45)[:, i:i+1], op=ALU.max)
                        v.drain()
                        v.tensor_scalar(out=stp(S_NTAU)[:, i:i+1],
                                        in0=stp(S_TAU)[:, i:i+1],
                                        scalar1=-1.0, scalar2=None,
                                        op0=ALU.mult)
                        V.tick(v.drain())
                        fm["tau"] = V.n
                    block.vector(p4_fin_v)

                    def p4_exp(s, i=i, Lk=Lk, base=base, fm=fm):
                        s.wait_ge(V.h, fm["tau"])
                        if "Pslot" in p4_ps_free:
                            s.wait_ge(T.h, p4_ps_free["Pslot"])
                        s.activation(out=AB_[:, PB:PB+Lk],
                                     in_=AF_[:, base:base+Lk],
                                     func=AF.Exp,
                                     bias=stp(S_NTAU)[:, i:i+1],
                                     scale=1.0)
                        SC.tick(s.drain())
                        fm["exp"] = SC.n
                    block.scalar(p4_exp)

                    def p4_maskz(v, i=i, Lk=Lk, fm=fm):
                        v.wait_ge(SC.h, fm["exp"])
                        v.scalar_tensor_tensor(
                            out=AB_[:, PB:PB+Lk], in0=AB_[:, PB:PB+Lk],
                            scalar=1.0, in1=AB_[:, PB:PB+Lk],
                            op0=ALU.is_ge, op1=ALU.mult)
                        V.tick(v.drain())
                        fm["pm"] = V.n
                    block.vector(p4_maskz)

                    ngr = (Lk + 511) // 512
                    for g in range(ngr):
                        w = min(512, Lk - g * 512)
                        nu = (w + 127) // 128
                        ps = psE if g % 2 == 0 else psF

                        def p4_ptr(t, g=g, nu=nu, ps=ps, fm=fm):
                            t.wait_ge(V.h, fm["pm"])
                            if ("ptev", g - 2) in fm:
                                t.wait_ge(V.h, fm[("ptev", g - 2)])
                            ins = None
                            for u in range(nu):
                                c = g * 4 + u
                                ins = t.transpose(
                                    out=ps[:, u*128:(u+1)*128],
                                    in_=AB_[:, PB + c*128:PB + (c+1)*128],
                                    identity=identb[:])
                            T.tick(ins)
                            fm[("ptr", g)] = T.n
                        block.tensor(p4_ptr)

                        def p4_ptev(v, g=g, w=w, ps=ps, fm=fm):
                            v.wait_ge(T.h, fm[("ptr", g)])
                            V.tick(v.tensor_copy(
                                out=AB_[:, PT + g*512:PT + g*512 + w],
                                in_=ps[:, 0:w]))
                            fm[("ptev", g)] = V.n
                        block.vector(p4_ptev)

                    def p4_av(t, qt=qt, Lk=Lk, kvh=kvh, fm=fm, ngr=ngr):
                        t.wait_ge(V.h, fm[("ptev", ngr - 1)])
                        p4_ps_free["Pslot"] = T.n
                        if "avev" in p4_ps_free:
                            t.wait_ge(V.h, p4_ps_free["avev"])
                        nkc = Lk // 128
                        ins = None
                        for kc in range(nkc):
                            ins = t.matmul(
                                out=psC[:, 0:VTW],
                                lhsT=AB_[:, PT + kc*128:PT + (kc+1)*128],
                                rhs=AB_[:, VTT + (kvh*16 + kc)*VTW:
                                        VTT + (kvh*16 + kc)*VTW + VTW],
                                start=(kc == 0), stop=(kc == nkc - 1))
                        T.tick(ins)
                        fm["av"] = T.n
                    block.tensor(p4_av)

                    def p4_avev(v, i=i, fm=fm):
                        v.wait_ge(T.h, fm["av"])
                        nc.vector.reciprocal(out=stp(S_RZ)[:, i:i+1],
                                             in_=psC[:, 128:129])
                        v.drain()
                        v.tensor_scalar(out=AB_[:, ATQ:ATQ+128],
                                        in0=psC[:, 0:128],
                                        scalar1=stp(S_RZ)[:, i:i+1],
                                        scalar2=None, op0=ALU.mult)
                        V.tick(v.drain())
                        p4_ps_free["avev"] = V.n
                        fm["avev"] = V.n
                    block.vector(p4_avev)

                    def p4_atr(t, fm=fm):
                        t.wait_ge(V.h, fm["avev"])
                        if "atev" in p4_ps_free:
                            t.wait_ge(V.h, p4_ps_free["atev"])
                        T.tick(t.transpose(out=psH[:], in_=AB_[:, ATQ:ATQ+128],
                                           identity=identb[:]))
                        fm["atr"] = T.n
                    block.tensor(p4_atr)

                    def p4_atev(v, qt=qt, ath=ath, hbuf=hbuf, fm=fm):
                        v.wait_ge(T.h, fm["atr"])
                        if qt == 0 and ("ath", hbuf) in ath_free:
                            v.wait_ge(SP.h, ath_free[("ath", hbuf)])
                        V.tick(v.tensor_copy(out=ath[:, qt*128:(qt+1)*128],
                                             in_=psH[:]))
                        p4_ps_free["atev"] = V.n
                        fm["atev"] = V.n
                        sel_mark[("atev", qt)] = V.n
                    block.vector(p4_atev)

                if DEBUG:
                    def p4_dbg(g_, h=h, half=half, sel_mark=sel_mark,
                               tiles=tiles):
                        g_.wait_ge(V.h, V.n)
                        SP.tick(g_.dma_start(out=dbgst_h[h, half], in_=st[:]),
                                16)
                    block.gpsimd(p4_dbg)
                    dbg_mark = SP.n

                    def p4_dbgw(v):
                        v.wait_ge(SP.h, dbg_mark)
                    block.vector(p4_dbgw)

            qh_free[("qh", hbuf)] = T.n

            def p4_spill(g_, h=h, hbuf=hbuf, ath=ath, sel_mark=sel_mark):
                g_.wait_ge(V.h, sel_mark[("atev", NQT - 1)])
                SP.tick(g_.dma_start(out=atsp_h[h], in_=ath[:]), 16)
            block.gpsimd(p4_spill)
            ath_free[("ath", hbuf)] = SP.n

    sp_after_p4 = SP.n
    v_after_p4 = V.n
    t_after_p4 = T.n

    WS5 = (0, 512)
    XSL = (1024, 1536)
    OSB = (2048, 2560)
    PP5 = 4096
    G1F = 8192
    G2F = 8704
    WEF = 0
    WEO = 16384
    XN8 = 32768
    AT8 = 40960

    pm = {}

    for c in range(32):
        def p5_pre_sync(sync, c=c, pm=pm):
            if c == 0:
                sync.wait_ge(V.h, v_after_p4)
            if c >= 2:
                sync.wait_ge(V.h, pm[("red", c - 2)])
            dst = AF_[:, PP5 + (c % 2) * D:PP5 + (c % 2) * D + D]
            LD.tick(sync.dma_start(out=dst, in_=wft_h[c*128:(c+1)*128, :]), 16)
            pm[("wld", c)] = LD.n
        block.sync(p5_pre_sync)

        def p5_pre_v(v, c=c, pm=pm):
            v.wait_ge(LD.h, pm[("wld", c)])
            srcp = AF_[:, PP5 + (c % 2) * D:PP5 + (c % 2) * D + D]
            v.tensor_reduce(out=cx(C_TMPA), in_=srcp,
                            axis=AX.X, op=ALU.max, apply_absolute_value=True)
            v.drain()
            if c == 0:
                V.tick(v.tensor_copy(out=cx(C_WMX), in_=cx(C_TMPA)))
            else:
                V.tick(v.tensor_tensor(out=cx(C_WMX), in0=cx(C_WMX),
                                       in1=cx(C_TMPA), op=ALU.max))
            v.drain()
            pm[("red", c)] = V.n
        block.vector(p5_pre_v)

    def p5_pre_vb(v, pm=pm):
        v.tensor_reduce(out=cx(C_BMX), in_=st[:, 480:512], axis=AX.X,
                        op=ALU.max, apply_absolute_value=True)
        V.tick(v.drain())
        pm["bmx"] = V.n
    block.vector(p5_pre_vb)

    def p5_pre_t(t, pm=pm):
        t.wait_ge(V.h, pm["bmx"])
        t.transpose(out=psA[0:1, 0:128], in_=cx(C_WMX), identity=identf[:])
        T.tick(t.transpose(out=psA[0:1, 128:256], in_=cx(C_BMX),
                           identity=identf[:]))
        pm["tr"] = T.n
    block.tensor(p5_pre_t)

    def p5_pre_v2(v, pm=pm):
        v.wait_ge(T.h, pm["tr"])
        v.reduce_max(out=AF_[0:1, 0:1], in_=psA[0:1, 0:128], axis=AX.X)
        v.reduce_max(out=AF_[0:1, 1:2], in_=psA[0:1, 128:256], axis=AX.X)
        V.tick(v.drain())
        pm["mx"] = V.n
    block.vector(p5_pre_v2)

    def p5_pre_t2(t, pm=pm):
        t.wait_ge(V.h, pm["mx"])
        t.matmul(out=psG[:, 0:1], lhsT=onesf[:], rhs=AF_[0:1, 0:1],
                 start=True, stop=True)
        t.matmul(out=psG[:, 1:2], lhsT=onesf[:], rhs=AF_[0:1, 1:2],
                 start=True, stop=True)
        t.matmul(out=psG[:, 2:3], lhsT=onesf[:], rhs=sc11[0:1, 0:1],
                 start=True, stop=True)
        T.tick(t.matmul(out=psG[:, 3:4], lhsT=onesf[:], rhs=sc11[0:1, 1:2],
                        start=True, stop=True))
        pm["bc"] = T.n
    block.tensor(p5_pre_t2)

    def p5_pre_v3(v, pm=pm):
        v.wait_ge(T.h, pm["bc"])
        v.tensor_tensor(out=cx(C_DLT), in0=psG[:, 0:1], in1=cx(C_R20),
                        op=ALU.mult)
        v.tensor_tensor(out=cx(C_DLTB), in0=psG[:, 1:2], in1=cx(C_R20),
                        op=ALU.mult)
        v.tensor_scalar(out=cx(C_WSP), in0=psG[:, 2:3], scalar1=0.5,
                        scalar2=None, op0=ALU.mult)
        v.tensor_scalar(out=cx(C_BSP), in0=psG[:, 3:4], scalar1=0.5,
                        scalar2=None, op0=ALU.mult)
        v.drain()
        v.tensor_scalar(out=cx(C_NDLT), in0=cx(C_DLT), scalar1=-1.0,
                        scalar2=None, op0=ALU.mult)
        v.tensor_scalar(out=cx(C_NDLTB), in0=cx(C_DLTB), scalar1=-1.0,
                        scalar2=None, op0=ALU.mult)
        V.tick(v.drain())
        pm["dlt"] = V.n
    block.vector(p5_pre_v3)

    def p5_bt_s(s, pm=pm):
        s.wait_ge(V.h, pm["dlt"])
        s.activation(out=AF_[:, 2048:2080], in_=st[:, 480:512], func=AF.Sign,
                     bias=cx(C_NDLTB), scale=1.0)
        s.activation(out=AF_[:, 2080:2112], in_=st[:, 480:512],
                     func=AF.Sign, bias=cx(C_DLTB), scale=1.0)
        SC.tick(s.drain())
        pm["bsgn"] = SC.n
    block.scalar(p5_bt_s)

    def p5_bt_v(v, pm=pm):
        v.wait_ge(SC.h, pm["bsgn"])
        v.tensor_tensor(out=AF_[:, 2048:2080], in0=AF_[:, 2048:2080],
                        in1=AF_[:, 2080:2112], op=ALU.add)
        v.drain()
        v.tensor_scalar(out=AF_[:, 2048:2080], in0=AF_[:, 2048:2080],
                        scalar1=cx(C_BSP), scalar2=None, op0=ALU.mult)
        V.tick(v.drain())
        pm["btm"] = V.n
    block.vector(p5_bt_v)

    def p5_bt_t(t, pm=pm):
        t.wait_ge(V.h, pm["btm"])
        T.tick(t.transpose(out=psG[0:32, 0:128], in_=AF_[:, 2048:2080],
                           identity=identf[:]))
        pm["bttr"] = T.n
    block.tensor(p5_bt_t)

    def p5_bt_v2(v, pm=pm):
        v.wait_ge(T.h, pm["bttr"])
        V.tick(v.tensor_copy(out=bts[0:32, :], in_=psG[0:32, 0:128]))
        pm["btev"] = V.n
    block.vector(p5_bt_v2)

    def p5_bt_g(g_, pm=pm):
        g_.wait_ge(V.h, pm["btev"])
        SP.tick(g_.dma_start(out=btb_h[:], in_=bts[0:32, :]), 16)
        pm["btsp"] = SP.n
    block.gpsimd(p5_bt_g)


    w5_free = {}
    act_free = {}
    osb_free = {}
    xsl_free = {}
    ps5_free = {}

    for ofe in range(8):
        wm = {}

        if ofe == 0:
            def p5_btsl(sync, wm=wm, pm=pm):
                sync.wait_ge(SP.h, pm["btsp"])
                LD.tick(sync.dma_start(
                    out=btrow_b.rearrange("q (a b) -> q a b", a=32),
                    in_=btb_h[:]), 16)
                pm["btrow"] = LD.n
            block.sync(p5_btsl)
        for c in range(32):
            for wi, wh in ((0, wft_h), (1, wot_h)):
                sb = (c * 2 + wi) % 2
                ws = AF_[:, WS5[sb]:WS5[sb] + 512]

                def p5_wld(sync, c=c, wi=wi, wh=wh, sb=sb, ws=ws, ofe=ofe,
                           wm=wm, pm=pm):
                    if ofe == 0 and c == 0 and wi == 0:
                        sync.wait_ge(V.h, pm["dlt"])
                    if ("ws", sb) in w5_free:
                        sync.wait_ge(V.h, w5_free[("ws", sb)])
                    LD.tick(sync.dma_start(
                        out=ws[:],
                        in_=wh[c*128:(c+1)*128, ofe*512:(ofe+1)*512]), 16)
                    wm[("ld", c, wi)] = LD.n
                block.sync(p5_wld)

                if wi == 0:
                    def p5_tern_s(s, c=c, ws=ws, wm=wm):
                        s.wait_ge(LD.h, wm[("ld", c, 0)])
                        s.activation(out=AF_[:, G1F:G1F+512], in_=ws[:],
                                     func=AF.Sign, bias=cx(C_NDLT), scale=1.0)
                        s.activation(out=AF_[:, G2F:G2F+512],
                                     in_=ws[:], func=AF.Sign,
                                     bias=cx(C_DLT), scale=1.0)
                        SC.tick(s.drain())
                        wm[("sgn", c)] = SC.n
                    block.scalar(p5_tern_s)

                    def p5_tern_v(v, c=c, sb=sb, wm=wm, ofe=ofe):
                        v.wait_ge(SC.h, wm[("sgn", c)])
                        if ("we", 0, c) in w5_free:
                            v.wait_ge(T.h, w5_free[("we", 0, c)])
                        v.tensor_tensor(out=AF_[:, G1F:G1F+512],
                                        in0=AF_[:, G1F:G1F+512],
                                        in1=AF_[:, G2F:G2F+512], op=ALU.add)
                        v.drain()
                        V.tick(v.tensor_scalar(
                            out=AB_[:, WEF + c*512:WEF + (c+1)*512],
                            in0=AF_[:, G1F:G1F+512], scalar1=cx(C_WSP),
                            scalar2=None, op0=ALU.mult))
                        w5_free[("ws", sb)] = V.n
                        wm[("wef", c)] = V.n
                    block.vector(p5_tern_v)
                else:
                    def p5_cast_v(v, c=c, sb=sb, wm=wm):
                        v.wait_ge(LD.h, wm[("ld", c, 1)])
                        if ("we", 1, c) in w5_free:
                            v.wait_ge(T.h, w5_free[("we", 1, c)])
                        ws = AF_[:, WS5[sb]:WS5[sb] + 512]
                        V.tick(v.tensor_copy(
                            out=AB_[:, WEO + c*512:WEO + (c+1)*512],
                            in_=ws[:]))
                        w5_free[("ws", sb)] = V.n
                        wm[("weo", c)] = V.n
                    block.vector(p5_cast_v)

        for tok8 in range(8):
            am = {}

            def p5_act_sync(sync, tok8=tok8, am=am, wm=wm, pm=pm):
                if "acts" in act_free:
                    sync.wait_ge(T.h, act_free["acts"])
                sync.wait_ge(SP.h, sp_after_p4)
                LD.tick(sync.dma_start(
                    out=AB_[:, XN8:XN8 + 32*256].rearrange(
                        "p (c t) -> p c t", c=32),
                    in_=xnsp_h[:, :, tok8*256:(tok8+1)*256].rearrange(
                        "c p t -> p c t")), 16)
                LD.tick(sync.dma_start(
                    out=AB_[:, AT8:AT8 + 32*256].rearrange(
                        "p (c t) -> p c t", c=32),
                    in_=atsp_h[:, :, tok8*256:(tok8+1)*256].rearrange(
                        "c p t -> p c t")), 16)
                am["acts"] = LD.n
            block.sync(p5_act_sync)

            for tokm in range(2):
                tok = tok8 * 256 + tokm * 128
                ps = psA if tokm == 0 else psB
                pk5 = ("ps5", id(ps))
                km = {}

                def p5_mm(t, ofe=ofe, tokm=tokm, ps=ps, pk5=pk5, am=am,
                          wm=wm, km=km, pm=pm):
                    t.wait_ge(LD.h, am["acts"])
                    t.wait_ge(LD.h, pm["btrow"])
                    t.wait_ge(V.h, wm[("wef", 31)])
                    t.wait_ge(V.h, wm[("weo", 31)])
                    if pk5 in ps5_free:
                        t.wait_ge(V.h, ps5_free[pk5])
                    ins = None
                    for c in range(32):
                        ins = t.matmul(
                            out=ps[:],
                            lhsT=AB_[:, XN8 + c*256 + tokm*128:
                                     XN8 + c*256 + tokm*128 + 128],
                            rhs=AB_[:, WEF + c*512:WEF + (c+1)*512],
                            start=(c == 0), stop=False)
                        ins = t.matmul(
                            out=ps[:],
                            lhsT=AB_[:, AT8 + c*256 + tokm*128:
                                     AT8 + c*256 + tokm*128 + 128],
                            rhs=AB_[:, WEO + c*512:WEO + (c+1)*512],
                            start=False, stop=False)
                    ins = t.matmul(out=ps[:], lhsT=onesb[:],
                                   rhs=btrow_b[0:1, ofe*512:(ofe+1)*512],
                                   start=False, stop=True)
                    T.tick(ins)
                    km["mm"] = T.n
                    act_free["acts"] = T.n
                    for c in range(32):
                        w5_free[("we", 0, c)] = T.n
                        w5_free[("we", 1, c)] = T.n
                block.tensor(p5_mm)

                sbx = tokm
                xsl = AF_[:, XSL[sbx]:XSL[sbx] + 512]
                osb = AF_[:, OSB[sbx]:OSB[sbx] + 512]

                def p5_xld(sync, ofe=ofe, tok=tok, xsl=xsl, sbx=sbx, km=km):
                    if ("xsl", sbx) in xsl_free:
                        sync.wait_ge(V.h, xsl_free[("xsl", sbx)])
                    LD.tick(sync.dma_start(
                        out=xsl[:],
                        in_=x_h[tok:tok+128, ofe*512:(ofe+1)*512]), 16)
                    km["x"] = LD.n
                block.sync(p5_xld)

                def p5_ev(v, xsl=xsl, osb=osb, ps=ps, pk5=pk5, sbx=sbx,
                          km=km):
                    v.wait_ge(T.h, km["mm"])
                    v.wait_ge(LD.h, km["x"])
                    if ("osb", sbx) in osb_free:
                        v.wait_ge(SP.h, osb_free[("osb", sbx)])
                    V.tick(v.scalar_tensor_tensor(
                        out=osb[:], in0=ps[:], scalar=1.0, in1=xsl[:],
                        op0=ALU.mult, op1=ALU.add))
                    ps5_free[pk5] = V.n
                    xsl_free[("xsl", sbx)] = V.n
                    km["osb"] = V.n
                block.vector(p5_ev)

                def p5_out(g_, ofe=ofe, tok=tok, osb=osb, sbx=sbx, km=km):
                    g_.wait_ge(V.h, km["osb"])
                    SP.tick(g_.dma_start(
                        out=out_h[tok:tok+128, ofe*512:(ofe+1)*512],
                        in_=osb[:]), 16)
                    osb_free[("osb", sbx)] = SP.n
                block.gpsimd(p5_out)

    sp_final = SP.n
    ld_final = LD.n

    def fin(g_):
        g_.wait_ge(SP.h, sp_final)
        g_.wait_ge(LD.h, ld_final)
    block.gpsimd(fin)

    stack.close()
    return nc


_PROG_CACHE = {}


def _get_program():
    if "nc" not in _PROG_CACHE:
        _PROG_CACHE["nc"] = build_program()
    return _PROG_CACHE["nc"]


def kernel(L0_attn_hidden_states, L0_attn_Wq, L0_attn_Wk, L0_attn_Wv,
           L0_attn_Wo, L0_ffn_W, L0_ffn_w_scale, L0_ffn_b, L0_ffn_b_scale):
    x = np.ascontiguousarray(np.asarray(L0_attn_hidden_states,
                                        dtype=np.float32))
    cst = _host_consts()
    shared = {
        "wqt": np.ascontiguousarray(np.asarray(L0_attn_Wq, np.float32).T),
        "wkt": np.ascontiguousarray(np.asarray(L0_attn_Wk, np.float32).T),
        "wvt": np.ascontiguousarray(np.asarray(L0_attn_Wv, np.float32).T),
        "wot": np.ascontiguousarray(np.asarray(L0_attn_Wo, np.float32).T),
        "wft": np.ascontiguousarray(np.asarray(L0_ffn_W, np.float32).T),
        "fb": np.ascontiguousarray(
            np.asarray(L0_ffn_b, np.float32).reshape(32, 128).T),
        "wsc": np.asarray(L0_ffn_w_scale, np.float32).reshape(1, 1),
        "bsc": np.asarray(L0_ffn_b_scale, np.float32).reshape(1, 1),
        "cost": cst["cost"], "sint": cst["sint"],
        "ident_bf": cst["ident_bf"], "ident_f32": cst["ident_f32"],
        "diagm": cst["diagm"], "iota8": cst["iota8"],
        "ones_bf": cst["ones_bf"], "ones_f": cst["ones_f"],
        "protT": cst["protT"],
        "s32col": cst["s32col"], "c20col": cst["c20col"],
        "lits": np.broadcast_to(np.array([1e-5, -0.5], np.float32),
                                (128, 2)).copy(),
        "z_t": cst["z_t"], "hz_t": cst["hz_t"], "n_t": cst["n_t"],
        "invLk_t": cst["invLk_t"], "Lk2_t": cst["Lk2_t"],
        "minflag_t": cst["minflag_t"], "minidx_t": cst["minidx_t"],
    }
    in_maps = []
    for b in range(8):
        m = dict(shared)
        m["x"] = np.ascontiguousarray(x[b])
        in_maps.append(m)
    nc = _get_program()
    res = run_bass_kernel_spmd(nc, in_maps, core_ids=list(range(8)))
    out = np.stack([np.asarray(r["out"], dtype=np.float32)
                    for r in res.results], axis=0)
    return out
